# version 13
# baseline (speedup 1.0000x reference)
import contextlib

import numpy as np
import ml_dtypes

import concourse.bass as bass
import concourse.bass_utils as _bu
import concourse.mybir as mybir
from concourse.bass_utils import run_bass_kernel_spmd

if not getattr(_bu, "_ldwopt_patched", False):
    _orig_run_command = _bu.run_command

    def _run_command_ldwopt(argv, **kw):
        argv = ["--enable-ldw-opt=true" if a == "--enable-ldw-opt=false" else a
                for a in argv]
        return _orig_run_command(argv, **kw)

    _bu.run_command = _run_command_ldwopt
    _bu._ldwopt_patched = True
    try:
        from concourse.compiler_utils import get_compiler_flags, set_compiler_flags
        set_compiler_flags([f.replace("--enable-ldw-opt=false",
                                      "--enable-ldw-opt=true")
                            for f in get_compiler_flags()])
    except Exception:
        pass

F32 = mybir.dt.float32
BF16 = mybir.dt.bfloat16
AF = mybir.ActivationFunctionType
ALU = mybir.AluOpType
AXX = mybir.AxisListType.X

NCORES = 8
T = 1024
D = 2048
NG = 16
NE = 8
NFT = 8
NH = 16
EG = 2
NGRP = NE // EG
AG = 2
NAG = NG // AG
EPS = 1e-5


def build_program(upto="all"):
    import os as _os
    STRIP_H = _os.environ.get("STRIP_H", "0") == "1"
    STRIP_SILU = _os.environ.get("STRIP_SILU", "0") == "1"
    STRIP_DMA = _os.environ.get("STRIP_DMA", "0") == "1"
    nc = bass.Bass("TRN2", target_bir_lowering=False)

    x_in = nc.dram_tensor("x", [T, D], F32, kind="ExternalInput").ap()
    wro_d = nc.dram_tensor("w_router", [D, NE], F32, kind="ExternalInput").ap()
    w1_d = nc.dram_tensor("w1", [NE, D, NFT * 128], F32, kind="ExternalInput").ap()
    w3_d = nc.dram_tensor("w3", [NE, D, NFT * 128], F32, kind="ExternalInput").ap()
    w2_d = nc.dram_tensor("w2", [NE, NFT * 128, D], F32, kind="ExternalInput").ap()
    wi_d = nc.dram_tensor("wi", [NH, 128, 128], F32, kind="ExternalInput").ap()
    bi_d = nc.dram_tensor("bi", [NH, 128], F32, kind="ExternalInput").ap()
    wr_d = nc.dram_tensor("wr", [NH, 128, 128], F32, kind="ExternalInput").ap()
    br_d = nc.dram_tensor("br", [NH, 128], F32, kind="ExternalInput").ap()
    rp_d = nc.dram_tensor("rp", [NH, 128], F32, kind="ExternalInput").ap()
    eye_d = nc.dram_tensor("eye", [128, 128], F32, kind="ExternalInput").ap()
    flag_d = nc.dram_tensor("flag0", [128, 2], F32, kind="ExternalInput").ap()
    oneh_d = nc.dram_tensor("onehot", [128, AG, NCORES], F32, kind="ExternalInput").ap()
    bce_d = nc.dram_tensor("bce", [NE, NE * 128], F32, kind="ExternalInput").ap()
    out_d = nc.dram_tensor("out", [T, D], F32, kind="ExternalOutput").ap()

    ccin = [nc.dram_tensor(f"ccin{g}", [128, 2 * AG], F32, kind="Internal").ap()
            for g in range(NAG)]
    ccout = [nc.dram_tensor(f"ccout{g}", [NCORES * 128, 2 * AG], F32, kind="Internal").ap()
             for g in range(NAG)]

    st = contextlib.ExitStack()
    with st:
        def sb(name, shape, dt=F32):
            return st.enter_context(nc.sbuf_tensor(name, list(shape), dt))

        xr = sb("xr", (128, NG, T))
        zb = sb("zb", (128, NG, T), BF16)
        AR = sb("AR", (128, 24576))
        sg = sb("sg", (128, T))
        eye_sb = sb("eye_sb", (128, 128))
        bce_sb = sb("bce_sb", (NE, NE * 128), BF16)
        wro_sb = sb("wro_sb", (128, NG, NE), BF16)
        ones_bf = sb("ones_bf", (128, 1), BF16)
        ones_row = sb("ones_row", (1, 128))
        bias_i = sb("bias_i", (128, NH))
        bias_r = sb("bias_r", (128, NH))
        msp = sb("msp", (128, NH))
        tmp16 = sb("tmp16", (128, NH))
        brow = sb("brow", (128, NH))
        flag = sb("flag", (128, 2))
        oneh = sb("oneh", (128, AG, NCORES))
        lt_sb = sb("lt_sb", (128, NE, NE))
        wts_sb = sb("wts_sb", (128, NE, NE))
        m_sc = sb("m_sc", (128, NE))
        m2_sc = sb("m2_sc", (128, NE))
        t8a = sb("t8a", (128, NE, NE))
        cin_sb = sb("cin_sb", (128, 2 * AG))
        agb = [sb(f"agb{b}", (128, NCORES, 2 * AG)) for b in range(2)]
        cpre = sb("cpre", (128, AG, 9))
        csel = sb("csel", (128, AG, NCORES))
        myc = sb("myc", (128, AG, 1))
        ceps = sb("ceps", (128, 1))
        sgp = sb("sgp", (128, T))
        ones_col = sb("ones_col", (128, 1))

        def arf(lo, hi):
            return AR[:, lo:hi]

        def arb(lo, hi):
            return AR[:, lo:hi].bitcast(BF16)

        wbc_v = arb(0, 4096).rearrange("p (e t) -> p e t", e=NE)
        stg_v = arf(0, 4096).rearrange("p (b c) -> p b c", b=2)
        h_v = arb(4096, 12288).rearrange("p (e f t) -> p e f t", e=EG, f=NFT)
        w13_v = arb(12288, 20480).rearrange(
            "p (b w gc j) -> p b w gc j", b=4, w=2, gc=NG)
        w2c_v = arb(20480, 24576).rearrange(
            "p (b e gf c) -> p b e gf c", b=4, e=EG, gf=NFT)
        R0 = AR[0:1, 4096:5120]
        R1 = AR[0:1, 5120:6144]
        R2 = AR[0:1, 6144:7168]
        xb_v = arb(8192, 9216).rearrange("p (b t) -> p b t", b=2)
        sq_v = arb(9216, 10240).rearrange("p (b t) -> p b t", b=2)
        wT_v = AR[0:NE, 4096:5120]
        wT2_v = AR[0:NE, 5120:5632].bitcast(BF16)
        z32_v = arf(0, 2048).rearrange("p (i t) -> p i t", i=AG)
        gx_v = arf(2048, 4096).rearrange("p (i t) -> p i t", i=AG)
        sa_v = arf(4096, 6144).rearrange("p (i t) -> p i t", i=AG)
        a_v = arf(6144, 8192).rearrange("p (i t) -> p i t", i=AG)
        ml_v = arf(8192, 10240).rearrange("p (i t) -> p i t", i=AG)
        y_v = [arf(10240, 12288).rearrange("p (i t) -> p i t", i=AG),
               arf(12288, 14336).rearrange("p (i t) -> p i t", i=AG)]
        ac_v = [arf(14336, 16384).rearrange("p (i t) -> p i t", i=AG),
                arf(16384, 18432).rearrange("p (i t) -> p i t", i=AG)]
        ab_v = arf(18432, 19456)
        bb_v = arf(19456, 20480)
        wi_v = arb(20480, 21504).rearrange("p (h j) -> p h j", h=NH)
        wr_v = arb(21504, 22528).rearrange("p (h j) -> p h j", h=NH)
        zeros_t = arf(22528, 23552)
        ob_v = zb[:, 0:4, :].bitcast(F32).rearrange(
            "p a t -> p (a t)").rearrange("p (b ti j) -> p b ti j", b=2, ti=NCORES)
        scr_v = zb[:, 15, :].bitcast(F32)

        ps = [st.enter_context(nc.psum_tensor(f"ps{i}", [128, T], F32)) for i in range(4)]

        class Ctr:
            def __init__(self, name):
                self.h = st.enter_context(nc.semaphore(name))
                self.n = 0

            def inc(self, inst, k=1):
                inst.then_inc(self.h, k)
                self.n += k
                return inst

        s_in = Ctr("s_in")
        s_w = Ctr("s_w")
        s_mm = Ctr("s_mm")
        s_sc = Ctr("s_sc")
        s_ev = Ctr("s_ev")
        s_gp = Ctr("s_gp")
        s_cc = Ctr("s_cc")
        s_out = Ctr("s_out")
        s_in2 = Ctr("s_in2")
        s_sg = Ctr("s_sg")

        block = st.enter_context(nc.Block())

        final_ct_ev = [None] * NG
        MM_GATES_LAST = [0]
        mm3_stats = [0]

        @block.sync
        def _(sync):
            s_in.inc(sync.dma_start(out=eye_sb[:], in_=eye_d[:]), 16)
            s_in.inc(sync.dma_start(out=flag[:], in_=flag_d[:]), 16)
            s_in.inc(sync.dma_start(out=oneh[:], in_=oneh_d[:]), 16)
            s_in.inc(sync.dma_start(out=scr_v[0:16, 0:128], in_=bi_d[:]), 16)
            s_in.inc(sync.dma_start(out=scr_v[0:16, 128:256], in_=br_d[:]), 16)
            s_in.inc(sync.dma_start(out=scr_v[0:16, 256:384], in_=rp_d[:]), 16)
        SIN_CONST = s_in.n

        @block.gpsimd
        def _(g):
            s_w.inc(g.dma_start(out=bce_sb[:], in_=bce_d[:]), 16)
            s_w.inc(g.dma_start(out=wro_sb[:],
                                in_=wro_d.rearrange("(gc p) e -> p gc e", p=128)), 16)
        SW_CONST = s_w.n

        @block.sync
        def _(sync):
            for ti in range(0, 8, 2):
                if ti >= 2:
                    sync.wait_ge(s_ev.h, 4 * (ti - 1))
                s_in.inc(sync.dma_start(out=stg_v[:, 0, :],
                                        in_=x_in[128 * ti:128 * (ti + 1), :]), 16)

        @block.scalar
        def _(sc):
            for ti in range(1, 8, 2):
                if ti >= 2:
                    sc.wait_ge(s_ev.h, 4 * (ti - 1))
                s_in2.inc(sc.dma_start(out=stg_v[:, 1, :],
                                       in_=x_in[128 * ti:128 * (ti + 1), :]), 16)

        @block.tensor
        def _(t):
            t.wait_ge(s_in.h, SIN_CONST)
            t.transpose(ps[0][:, 0:16], scr_v[0:16, 0:128], eye_sb[0:16, 0:16])
            t.transpose(ps[0][:, 16:32], scr_v[0:16, 128:256], eye_sb[0:16, 0:16])
            i = t.transpose(ps[0][:, 32:48], scr_v[0:16, 256:384], eye_sb[0:16, 0:16])
            s_mm.inc(i)
            for ti in range(8):
                if ti % 2 == 0:
                    t.wait_ge(s_in.h, SIN_CONST + 16 * (ti // 2 + 1))
                else:
                    t.wait_ge(s_in2.h, 16 * (ti // 2 + 1))
                for q in range(4):
                    Q = 4 * ti + q
                    if Q >= 4:
                        t.wait_ge(s_ev.h, Q - 3)
                    bank = ps[1 + Q % 2]
                    off = 512 * ((Q % 4) // 2)
                    for j in range(4):
                        gi = 4 * q + j
                        i = t.transpose(bank[:, off + 128 * j:off + 128 * (j + 1)],
                                        stg_v[:, ti % 2, 128 * gi:128 * (gi + 1)],
                                        eye_sb[:])
                    s_mm.inc(i)
        MM_IN = s_mm.n
        assert MM_IN == 33

        @block.vector
        def _(v):
            v.wait_ge(s_mm.h, 1)
            v.tensor_copy(bias_i[:], ps[0][:, 0:16])
            v.tensor_copy(bias_r[:], ps[0][:, 16:32])
            v.tensor_copy(brow[:], ps[0][:, 32:48])
            v.drain()
            v.tensor_single_scalar(tmp16[:], brow[:], 0.0, ALU.is_ge)
            v.drain()
            v.tensor_scalar(tmp16[:], tmp16[:], 2.0, -1.0, ALU.mult, ALU.add)
            v.drain()
            v.tensor_mul(tmp16[:], brow[:], tmp16[:])
            v.drain()
            v.memset(cpre[:], 0.0)
            v.memset(ones_bf[:], 1.0)
            v.memset(ones_row[:], 1.0)
            v.memset(ones_col[:], 1.0)
            v.memset(ceps[:], EPS)
            for Q in range(32):
                ti, q = Q // 4, Q % 4
                v.wait_ge(s_mm.h, 1 + Q + 1)
                bank = ps[1 + Q % 2]
                off = 512 * ((Q % 4) // 2)
                i = v.tensor_copy(
                    xr[:, 4 * q:4 * q + 4, 128 * ti:128 * (ti + 1)],
                    bank[:, off:off + 512].rearrange("p (j c) -> p j c", j=4))
                s_ev.inc(i)
        EV_IN = s_ev.n
        assert EV_IN == 32

        @block.scalar
        def _(s):
            s.wait_ge(s_ev.h, EV_IN)
            s.activation(msp[:], tmp16[:], AF.Exp, scale=-1.0)
            s.activation(tmp16[:], msp[:], AF.Ln, bias=1.0)
            i = s.drain()
            s_sc.inc(i)
        SC_MSP = s_sc.n

        @block.vector
        def _(v):
            v.wait_ge(s_sc.h, SC_MSP)
            v.tensor_relu(msp[:], brow[:])
            v.drain()
            v.tensor_add(msp[:], msp[:], tmp16[:])
            v.drain()
            v.tensor_scalar_mul(msp[:], msp[:], -8.0)
            i = v.drain()
            s_ev.inc(i)

        def emit_ln(gate_ev_scalar):
            mm0, sc0, ev0 = s_mm.n, s_sc.n, s_ev.n

            @block.vector
            def _(v):
                for g in range(NG):
                    if g >= 2:
                        v.wait_ge(s_mm.h, mm0 + g - 1)
                    i = v.tensor_copy(xb_v[:, g % 2, :], xr[:, g, :])
                    s_ev.inc(i)

            @block.scalar
            def _(s):
                s.wait_ge(s_ev.h, gate_ev_scalar)
                for g in range(NG):
                    if g >= 2:
                        s.wait_ge(s_mm.h, mm0 + g - 1)
                    i = s.activation(sq_v[:, g % 2, :], xr[:, g, :], AF.Square)
                    s_sc.inc(i)

            @block.tensor
            def _(t):
                for g in range(NG):
                    t.wait_ge(s_ev.h, ev0 + g + 1)
                    t.wait_ge(s_sc.h, sc0 + g + 1)
                    for half in range(2):
                        sl = slice(512 * half, 512 * half + 512)
                        t.matmul(ps[2][0:1, sl], ones_bf[:], xb_v[:, g % 2, sl],
                                 start=(g == 0), stop=(g == NG - 1),
                                 skip_group_check=True)
                        i = t.matmul(ps[3][0:1, sl], ones_bf[:], sq_v[:, g % 2, sl],
                                     start=(g == 0), stop=(g == NG - 1),
                                     skip_group_check=True)
                    s_mm.inc(i)
            mm_stats = s_mm.n

            return emit_ln_tail(mm_stats)

        def emit_ln_tail(mm_stats):
            @block.vector
            def _(v):
                v.wait_ge(s_mm.h, mm_stats)
                v.tensor_scalar_mul(R0, ps[2][0:1, :], 1.0 / D)
                v.tensor_scalar_mul(R1, ps[3][0:1, :], 1.0 / D)
                v.drain()
                v.tensor_mul(R2, R0, R0)
                v.drain()
                v.tensor_sub(R1, R1, R2)
                i = v.drain()
                s_ev.inc(i)
            ev_var = s_ev.n

            @block.scalar
            def _(s):
                s.wait_ge(s_ev.h, ev_var)
                s.activation(R2, R1, AF.Ln, bias=ceps[0:1, 0:1])
                s.activation(R1, R2, AF.Exp, scale=-0.5)
                i = s.drain()
                s_sc.inc(i)
            sc_alpha = s_sc.n

            @block.vector
            def _(v):
                v.wait_ge(s_sc.h, sc_alpha)
                v.scalar_tensor_tensor(R2, R0, -1.0, R1, ALU.mult, ALU.mult)
                i = v.drain()
                s_ev.inc(i)
            ev_beta = s_ev.n

            @block.tensor
            def _(t):
                t.wait_ge(s_ev.h, ev_beta)
                for half in range(2):
                    sl = slice(512 * half, 512 * half + 512)
                    t.matmul(ps[0][:, sl], ones_row[:], R1[0:1, 512 * half:512 * half + 512],
                             start=True, stop=True)
                    i = t.matmul(ps[1][:, sl], ones_row[:], R2[0:1, 512 * half:512 * half + 512],
                                 start=True, stop=True)
                s_mm.inc(i)
            mm_bc = s_mm.n

            @block.vector
            def _(v):
                v.wait_ge(s_mm.h, mm_bc)
                v.tensor_copy(ab_v, ps[0][:])
                i = v.tensor_copy(bb_v, ps[1][:])
                s_ev.inc(i)
            return s_ev.n

        def emit_ln_apply_all(ev_ab):
            ev0 = s_ev.n
            gp0 = s_gp.n

            @block.vector
            def _(v):
                v.wait_ge(s_ev.h, ev_ab)
                for g in range(8):
                    v.tensor_mul(sg[:], xr[:, g, :], ab_v)
                    i = v.tensor_add(zb[:, g, :], sg[:], bb_v)
                    s_ev.inc(i)

            @block.gpsimd
            def _(g):
                g.wait_ge(s_ev.h, ev_ab)
                for g2 in range(8, NG):
                    g.tensor_mul(sgp[:], xr[:, g2, :], ab_v)
                    i = g.tensor_add(zb[:, g2, :], sgp[:], bb_v)
                    s_gp.inc(i, 1)
            return ev0, gp0

        def emit_router(z_ready):
            ev0, gp0 = z_ready

            @block.tensor
            def _(t):
                t.wait_ge(s_w.h, SW_CONST)
                for g in range(NG):
                    if g < 8:
                        t.wait_ge(s_ev.h, ev0 + g + 1)
                    else:
                        t.wait_ge(s_gp.h, gp0 + g - 7)
                    for half in range(2):
                        sl = slice(512 * half, 512 * half + 512)
                        i = t.matmul(ps[2][0:8, sl], wro_sb[:, g, :], zb[:, g, sl],
                                     start=(g == 0), stop=(g == NG - 1),
                                     skip_group_check=True)
                    s_mm.inc(i)
            mm_lg = s_mm.n

            @block.vector
            def _(v):
                v.wait_ge(s_mm.h, mm_lg)
                i = v.tensor_copy(wT_v, ps[2][0:8, :])
                s_ev.inc(i)
            ev_lg = s_ev.n

            @block.tensor
            def _(t):
                t.wait_ge(s_ev.h, ev_lg)
                for j in range(8):
                    i = t.transpose(ps[3][:, 8 * j:8 * (j + 1)],
                                    wT_v[0:8, 128 * j:128 * (j + 1)], eye_sb[0:8, 0:8])
                s_mm.inc(i)
            mm_tp = s_mm.n

            @block.vector
            def _(v):
                v.wait_ge(s_mm.h, mm_tp)
                v.tensor_copy(lt_sb[:].rearrange("p a b -> p (a b)"), ps[3][:, 0:64])
                v.drain()
                for j in range(8):
                    v.reduce_max(m_sc[:, j:j + 1], lt_sb[:, j, :], axis=AXX, negate=True)
                v.drain()
                v.tensor_scalar_mul(m2_sc[:], m_sc[:], -1.0)
                i = v.drain()
                s_ev.inc(i)
            ev_m1 = s_ev.n

            @block.scalar
            def _(s):
                s.wait_ge(s_ev.h, ev_m1)
                for j in range(8):
                    s.activation(wts_sb[:, j, :], lt_sb[:, j, :], AF.Exp,
                                 bias=m_sc[:, j:j + 1])
                i = s.drain()
                s_sc.inc(i)
            sc_exp = s_sc.n

            @block.vector
            def _(v):
                v.wait_ge(s_sc.h, sc_exp)
                for j in range(8):
                    v.tensor_single_scalar(t8a[:, j, :], lt_sb[:, j, :],
                                           m2_sc[:, j:j + 1], ALU.is_ge)
                v.drain()
                for j in range(8):
                    v.scalar_tensor_tensor(t8a[:, j, :], t8a[:, j, :], -1e30,
                                           lt_sb[:, j, :], ALU.mult, ALU.add)
                v.drain()
                for j in range(8):
                    v.reduce_max(m2_sc[:, j:j + 1], t8a[:, j, :], axis=AXX)
                v.drain()
                for j in range(8):
                    v.tensor_single_scalar(t8a[:, j, :], lt_sb[:, j, :],
                                           m2_sc[:, j:j + 1], ALU.is_ge)
                v.drain()
                for j in range(8):
                    v.tensor_mul(wts_sb[:, j, :], wts_sb[:, j, :], t8a[:, j, :])
                v.drain()
                for j in range(8):
                    v.reduce_sum(m2_sc[:, j:j + 1], wts_sb[:, j, :], axis=AXX)
                v.drain()
                v.reciprocal(m2_sc[:], m2_sc[:])
                v.drain()
                v.tensor_scalar_mul(m2_sc[:], m2_sc[:], 0.5)
                v.drain()
                for j in range(8):
                    v.tensor_single_scalar(wts_sb[:, j, :], wts_sb[:, j, :],
                                           m2_sc[:, j:j + 1], ALU.mult)
                i = v.drain()
                s_ev.inc(i)
            ev_w = s_ev.n

            @block.tensor
            def _(t):
                t.wait_ge(s_ev.h, ev_w)
                for j in range(8):
                    i = t.transpose(ps[3][0:8, 128 * j:128 * (j + 1)],
                                    wts_sb[:, j, :], eye_sb[:])
                s_mm.inc(i)
            mm_tpb = s_mm.n

            @block.vector
            def _(v):
                v.wait_ge(s_mm.h, mm_tpb)
                i = v.tensor_copy(wT2_v, ps[3][0:8, :])
                s_ev.inc(i)
            ev_wT2 = s_ev.n

            @block.tensor
            def _(t):
                t.wait_ge(s_ev.h, ev_wT2)
                for e in range(NE):
                    if e >= 2:
                        t.wait_ge(s_ev.h, ev_wT2 + e - 1)
                    p = ps[e % 2]
                    for half in range(2):
                        sl = slice(512 * half, 512 * half + 512)
                        i = t.matmul(p[:, sl],
                                     bce_sb[0:8, 128 * e:128 * (e + 1)],
                                     wT2_v[0:8, sl], start=True, stop=True,
                                     skip_group_check=True)
                    s_mm.inc(i)

            @block.vector
            def _(v):
                for e in range(NE):
                    v.wait_ge(s_mm.h, mm_tpb + e + 1)
                    i = v.tensor_copy(wbc_v[:, e, :], ps[e % 2][:])
                    s_ev.inc(i)
            return s_ev.n

        def emit_ffn(call):
            for grp in range(NGRP):
                e0 = EG * grp
                mmG0, scG0, evG0, swG0 = s_mm.n, s_sc.n, s_ev.n, s_w.n

                @block.gpsimd
                def _(g):
                    for c in range(16):
                        ei, fc = c // 8, c % 8
                        if c < 4:
                            if grp == 0:
                                g.wait_ge(s_ev.h, evG0)
                            else:
                                g.wait_ge(s_mm.h, mmG0 - 16)
                        else:
                            g.wait_ge(s_mm.h, mmG0 + c - 3)
                        ngc = 1 if STRIP_DMA else 8
                        for wsel, wd in ((0, w1_d), (1, w3_d)):
                            src = wd[e0 + ei].rearrange("(gc p) f -> p gc f", p=128)
                            for gh in range(2):
                                i = g.dma_start(
                                    out=w13_v[:, c % 4, wsel, 8 * gh:8 * gh + ngc, :],
                                    in_=src[:, 8 * gh:8 * gh + ngc, 128 * fc:128 * fc + 128])
                                s_w.inc(i, 16)

                @block.tensor
                def _(t):
                    for f in range(16):
                        ei, ft = f // 8, f % 8
                        c = f
                        t.wait_ge(s_w.h, swG0 + 64 * (c + 1))
                        if f == 0:
                            t.wait_ge(s_ev.h, evG0 if grp == 0 else evG0 - 17)
                        elif f == 1:
                            t.wait_ge(s_ev.h, evG0)
                        else:
                            t.wait_ge(s_ev.h, evG0 + f - 1)
                        pg = ps[2] if f % 2 == 0 else ps[0]
                        pu = ps[3] if f % 2 == 0 else ps[1]
                        for wsel, ptgt in ((0, pg), (1, pu)):
                            for half in range(2):
                                sl = slice(512 * half, 512 * half + 512)
                                for gc in range(NG):
                                    i = t.matmul(ptgt[:, sl],
                                                 w13_v[:, c % 4, wsel, gc, :],
                                                 zb[:, gc, sl],
                                                 start=(gc == 0), stop=(gc == NG - 1))
                        s_mm.inc(i)

                @block.scalar
                def _(s):
                    for f in range(16):
                        s.wait_ge(s_mm.h, mmG0 + f + 1)
                        pg = ps[2] if f % 2 == 0 else ps[0]
                        if STRIP_SILU:
                            i = s.copy(m_sc[:, 0:1], m_sc[:, 0:1])
                        else:
                            i = s.activation(pg[:], pg[:], AF.Silu)
                        s_sc.inc(i)

                @block.vector
                def _(v):
                    for f in range(16):
                        ei, ft = f // 8, f % 8
                        v.wait_ge(s_sc.h, scG0 + f + 1)
                        if f >= 1:
                            v.wait_ge(s_ev.h, evG0 + f)
                        pg = ps[2] if f % 2 == 0 else ps[0]
                        pu = ps[3] if f % 2 == 0 else ps[1]
                        if STRIP_H:
                            i = v.tensor_copy(m2_sc[:, 0:1], m2_sc[:, 0:1])
                        else:
                            v.tensor_mul(sg[:], pg[:], wbc_v[:, e0 + ei, :])
                            i = v.tensor_mul(h_v[:, ei, ft, :], sg[:], pu[:])
                        s_ev.inc(i)

                mmY0, evY0, swY0 = s_mm.n, s_ev.n, s_w.n

                @block.gpsimd
                def _(g):
                    for tch in range(16):
                        if tch < 4:
                            if grp == 0 and call == 1:
                                g.wait_ge(s_ev.h, evY0 - 16)
                            elif grp > 0:
                                g.wait_ge(s_mm.h, mmY0 - 16)
                        else:
                            g.wait_ge(s_mm.h, mmY0 + tch - 3)
                        ngf = 1 if STRIP_DMA else 8
                        for ei in range(EG):
                            src = w2_d[e0 + ei].rearrange("(gf p) c -> p gf c", p=128)
                            i = g.dma_start(
                                out=w2c_v[:, tch % 4, ei, 0:ngf, :],
                                in_=src[:, 0:ngf, 128 * tch:128 * tch + 128])
                            s_w.inc(i, 16)

                @block.tensor
                def _(t):
                    for ct in range(NG):
                        tch = ct
                        t.wait_ge(s_w.h, swY0 + 32 * (tch + 1))
                        if ct < 2:
                            t.wait_ge(s_ev.h, evY0)
                        else:
                            t.wait_ge(s_ev.h, evY0 + ct - 1)
                        py = ps[0] if ct % 2 == 0 else ps[1]
                        for half in range(2):
                            sl = slice(512 * half, 512 * half + 512)
                            n = 0
                            for ei in range(EG):
                                for gf in range(NFT):
                                    i = t.matmul(py[:, sl],
                                                 w2c_v[:, tch % 4, ei, gf, :],
                                                 h_v[:, ei, gf, sl],
                                                 start=(n == 0), stop=(n == EG * NFT - 1))
                                    n += 1
                        s_mm.inc(i)

                @block.vector
                def _(v):
                    for ct in range(NG):
                        v.wait_ge(s_mm.h, mmY0 + ct + 1)
                        py = ps[0] if ct % 2 == 0 else ps[1]
                        i = v.tensor_add(xr[:, ct, :], xr[:, ct, :], py[:])
                        s_ev.inc(i)
                        if call == 1 and grp == NGRP - 1:
                            final_ct_ev[ct] = s_ev.n

        def emit_attn(ev_ab):
            MM_FFN1 = s_mm.n

            @block.gpsimd
            def _(g):
                g.wait_ge(s_mm.h, MM_FFN1)
                s_w.inc(g.dma_start(out=wi_v, in_=wi_d.rearrange("h d c -> d h c")), 16)
                s_w.inc(g.dma_start(out=wr_v, in_=wr_d.rearrange("h d c -> d h c")), 16)
            SW_WIWR = s_w.n

            @block.vector
            def _(v):
                v.wait_ge(s_ev.h, ev_ab)
                v.wait_ge(s_mm.h, MM_FFN1)
                v.memset(zeros_t, 0.0)
                for g in range(4):
                    v.tensor_mul(sg[:], xr[:, g, :], ab_v)
                    i = v.tensor_add(zb[:, g, :], sg[:], bb_v)
                    s_ev.inc(i)
            ev_zb_first = s_ev.n
            gp_zb = [0] * NAG

            fix = [None] * NAG
            ev_scan_prev = [0]
            gp_z32 = [0] * NAG
            gp_nx1_l = [0]

            sq_state = {"sc": 0, "mm_prev": 0}

            def emit_tile_stats(k):
                sqv = zb[:, 0, :]
                for g2 in (2 * k, 2 * k + 1):
                    @block.scalar
                    def _(s, g2=g2):
                        s.wait_ge(s_ev.h, fix_ev[k])
                        if sq_state["mm_prev"]:
                            s.wait_ge(s_mm.h, sq_state["mm_prev"])
                        i = s.activation(sqv, xr[:, g2, :], AF.Square)
                        s_sc.inc(i)
                    sq_state["sc"] = s_sc.n

                    @block.tensor
                    def _(t, g2=g2):
                        t.wait_ge(s_sc.h, sq_state["sc"])
                        t.wait_ge(s_ev.h, fix_ev[k])
                        for half in range(2):
                            sl = slice(512 * half, 512 * half + 512)
                            t.matmul(ps[2][0:1, sl], ones_col[:], xr[:, g2, sl],
                                     start=(g2 == 0), stop=(g2 == NG - 1),
                                     skip_group_check=True)
                            i = t.matmul(ps[3][0:1, sl], ones_bf[:], sqv[:, sl],
                                         start=(g2 == 0), stop=(g2 == NG - 1),
                                         skip_group_check=True)
                        s_mm.inc(i)
                    sq_state["mm_prev"] = s_mm.n
                if k == NAG - 1:
                    mm3_stats[0] = s_mm.n

            def emit_fixup(k):
                b, gA, gp_after = fix[k]

                @block.vector
                def _(v):
                    v.wait_ge(s_gp.h, gp_after)
                    for r in range(NCORES):
                        for i2 in range(AG):
                            v.scalar_tensor_tensor(
                                cpre[:, i2, r + 1:r + 2],
                                agb[b][:, r, 2 * i2:2 * i2 + 1],
                                cpre[:, i2, r:r + 1],
                                agb[b][:, r, 2 * i2 + 1:2 * i2 + 2],
                                ALU.mult, ALU.add)
                        v.drain()
                    v.tensor_mul(csel[:], cpre[:, :, 0:8], oneh[:])
                    v.drain()
                    v.reduce_sum(myc[:], csel[:], axis=AXX)
                    v.drain()
                    for i2 in range(AG):
                        g2 = gA + i2
                        v.scalar_tensor_tensor(y_v[b][:, i2, :], ac_v[b][:, i2, :],
                                               myc[:, i2, 0:1], y_v[b][:, i2, :],
                                               ALU.mult, ALU.add)
                        inst = v.tensor_add(xr[:, g2, :], xr[:, g2, :], y_v[b][:, i2, :])
                    s_ev.inc(inst)
                fix_ev[k] = s_ev.n

            fix_ev = [0] * NAG
            for k in range(NAG):
                b = k % 2
                gA = AG * k
                ev_apply0 = s_ev.n

                mmK0 = s_mm.n

                @block.tensor
                def _(t):
                    if k == 0:
                        t.wait_ge(s_w.h, SW_WIWR)
                        t.wait_ge(s_ev.h, ev_zb_first)
                    if k >= 2:
                        t.wait_ge(s_gp.h, gp_zb[k])
                    for i2 in range(AG):
                        g2 = gA + i2
                        tile_no = 2 * k + i2
                        if tile_no >= 1:
                            t.wait_ge(s_sg.h, tile_no)
                        pgx, pga = ps[0], ps[1]
                        for half in range(2):
                            sl = slice(512 * half, 512 * half + 512)
                            t.matmul(pgx[:, sl], wi_v[:, g2, :], zb[:, g2, sl],
                                     start=True, stop=True, skip_group_check=True)
                            inst = t.matmul(pga[:, sl], wr_v[:, g2, :], zb[:, g2, sl],
                                            start=True, stop=True, skip_group_check=True)
                        s_mm.inc(inst)
                MM_GATES_LAST[0] = s_mm.n

                @block.scalar
                def _(s):
                    if k > 0:
                        s.wait_ge(s_ev.h, ev_scan_prev[0])
                    for i2 in range(AG):
                        g2 = gA + i2
                        s.wait_ge(s_mm.h, mmK0 + i2 + 1)
                        s.activation(gx_v[:, i2, :], ps[0][:], AF.Sigmoid,
                                     bias=bias_i[:, g2:g2 + 1])
                        inst = s.activation(sa_v[:, i2, :], ps[1][:], AF.Sigmoid,
                                            bias=bias_r[:, g2:g2 + 1])
                        s_sg.inc(inst)

                @block.scalar
                def _(s):
                    for i2 in range(AG):
                        g2 = gA + i2
                        s.activation(a_v[:, i2, :], sa_v[:, i2, :], AF.Exp,
                                     scale=msp[:, g2:g2 + 1])
                    i = s.drain()
                    s_sc.inc(i)
                sc_a = s_sc.n

                @block.scalar
                def _(s):
                    fa = a_v.rearrange("p i t -> p (i t)")
                    fs = sa_v.rearrange("p i t -> p (i t)")
                    fm = ml_v.rearrange("p i t -> p (i t)")
                    s.activation(fm, fa, AF.Square)
                    inst = s.activation(fs, fm, AF.Relu, scale=-1.0, bias=1.0)
                    s_sc.inc(inst)
                sc_u = s_sc.n

                @block.scalar
                def _(s):
                    fs = sa_v.rearrange("p i t -> p (i t)")
                    fm = ml_v.rearrange("p i t -> p (i t)")
                    s.activation(fm, fs, AF.Ln)
                    inst = s.activation(fs, fm, AF.Exp, scale=0.5)
                    s_sc.inc(inst)
                sc_mult = s_sc.n

                gp0a = s_gp.n

                @block.gpsimd
                def _(g):
                    if k == 0:
                        g.wait_ge(s_ev.h, ev_ab)
                        for i2 in range(AG):
                            g.tensor_mul(sgp[:], xr[:, i2, :], ab_v)
                            i = g.tensor_add(z32_v[:, i2, :], sgp[:], bb_v)
                            s_gp.inc(i, 1)
                        gp_z32[0] = s_gp.n
                        for g2 in range(4, 8):
                            g.tensor_mul(sgp[:], xr[:, g2, :], ab_v)
                            i = g.tensor_add(zb[:, g2, :], sgp[:], bb_v)
                            s_gp.inc(i, 1)
                        gp_zb[2] = gp_zb[3] = s_gp.n
                    elif 1 <= k <= 4:
                        for g2 in range(2 * k + 6, 2 * k + 8):
                            g.tensor_mul(sgp[:], xr[:, g2, :], ab_v)
                            i = g.tensor_add(zb[:, g2, :], sgp[:], bb_v)
                            s_gp.inc(i, 1)
                        gp_zb[k + 3] = s_gp.n
                    g.wait_ge(s_sg.h, 2 * k + 2)
                    fz = z32_v.rearrange("p i t -> p (i t)")
                    fg = gx_v.rearrange("p i t -> p (i t)")
                    i = g.tensor_mul(fg, fz, fg)
                    s_gp.inc(i, 1)
                    gp_nx1_l[0] = s_gp.n
                    if k < NAG - 1:
                        for i2 in range(AG):
                            g.tensor_mul(sgp[:], xr[:, 2 * k + 2 + i2, :], ab_v)
                            i = g.tensor_add(z32_v[:, i2, :], sgp[:], bb_v)
                            s_gp.inc(i, 1)
                        gp_z32[k + 1] = s_gp.n

                @block.vector
                def _(v):
                    v.wait_ge(s_sc.h, sc_mult)
                    v.wait_ge(s_gp.h, gp_nx1_l[0])
                    for i2 in range(AG):
                        v.scalar_tensor_tensor(sa_v[:, i2, 0:1], sa_v[:, i2, 0:1],
                                               flag[:, 1:2], flag[:, 0:1],
                                               ALU.mult, ALU.add)
                    v.drain()
                    fg = gx_v.rearrange("p i t -> p (i t)")
                    fs = sa_v.rearrange("p i t -> p (i t)")
                    v.tensor_mul(fg, fg, fs)
                    for i2 in range(AG):
                        v.tensor_tensor_scan(y_v[b][:, i2, :], a_v[:, i2, :],
                                             gx_v[:, i2, :], 0.0, ALU.mult, ALU.add)
                        v.tensor_tensor_scan(ac_v[b][:, i2, :], a_v[:, i2, :],
                                             zeros_t, 1.0, ALU.mult, ALU.add)
                    v.drain()
                    for i2 in range(AG):
                        v.tensor_copy(cin_sb[:, 2 * i2:2 * i2 + 1],
                                      ac_v[b][:, i2, T - 1:T])
                        v.tensor_copy(cin_sb[:, 2 * i2 + 1:2 * i2 + 2],
                                      y_v[b][:, i2, T - 1:T])
                    inst = v.drain()
                    s_ev.inc(inst)
                ev_scan = s_ev.n
                ev_scan_prev[0] = ev_scan

                gp0 = s_gp.n

                @block.gpsimd
                def _(g):
                    g.wait_ge(s_ev.h, ev_scan)
                    s_gp.inc(g.dma_start(out=ccin[k][:], in_=cin_sb[:]), 16)
                    g.wait_ge(s_gp.h, gp0 + 16)
                    s_cc.inc(g.collective_compute(
                        "AllGather", ALU.bypass,
                        replica_groups=[list(range(NCORES))],
                        ins=[ccin[k][:].opt()], outs=[ccout[k][:].opt()]), 1)
                    g.wait_ge(s_cc.h, s_cc.n)
                    s_gp.inc(g.dma_start(
                        out=agb[b][:],
                        in_=ccout[k].rearrange("(r p) w -> p r w", p=128)), 16)
                fix[k] = (b, gA, s_gp.n)

                if k >= 1:
                    emit_fixup(k - 1)
                if k >= 2:
                    emit_tile_stats(k - 2)
            emit_fixup(NAG - 1)
            emit_tile_stats(NAG - 2)
            emit_tile_stats(NAG - 1)
            return s_ev.n

        dump_specs = []

        def dump(name, ap):
            dump_specs.append((name, ap))

        def finish_dumps():
            evq, mmq, scq, gpq = s_ev.n, s_mm.n, s_sc.n, s_gp.n
            outs = []
            for name, ap in dump_specs:
                shp = list(ap.shape)
                dt = ap.dtype
                d = nc.dram_tensor("dbg_" + name, shp, dt, kind="ExternalOutput").ap()
                outs.append((d, ap))

            @block.sync
            def _(sync):
                sync.wait_ge(s_ev.h, evq)
                sync.wait_ge(s_mm.h, mmq)
                sync.wait_ge(s_sc.h, scq)
                if gpq:
                    sync.wait_ge(s_gp.h, gpq)
                for d, ap in outs:
                    s_out.inc(sync.dma_start(out=d, in_=ap), 16)
                sync.wait_ge(s_out.h, s_out.n)

        if upto == "x0":
            dump("xr", xr[:])
            finish_dumps()
            return nc

        ev_ab1 = emit_ln(gate_ev_scalar=EV_IN)
        ev_z1 = emit_ln_apply_all(ev_ab1)
        if upto == "ln1":
            dump("zb", zb[:])
            dump("ab", ab_v)
            dump("bb", bb_v)
            dump("msp", msp[:])
            dump("tmp16", tmp16[:])
            dump("brow", brow[:])
            finish_dumps()
            return nc
        emit_router(ev_z1)
        if upto == "router1":
            dump("wts", wts_sb[:])
            dump("lt", lt_sb[:])
            dump("wbc", wbc_v)
            dump("wT2", wT2_v)
            finish_dumps()
            return nc
        emit_ffn(0)
        if upto == "ffn1":
            dump("xr", xr[:])
            finish_dumps()
            return nc

        ev_ab2 = emit_ln(gate_ev_scalar=s_ev.n)
        emit_attn(ev_ab2)
        if upto == "attn":
            dump("xr", xr[:])
            finish_dumps()
            return nc

        ev_ab3 = emit_ln_tail(mm3_stats[0])
        z3 = emit_ln_apply_all(ev_ab3)
        emit_router(z3)
        emit_ffn(1)

        mmO0 = s_mm.n
        evO0 = s_ev.n

        @block.tensor
        def _(t):
            for ct in range(NG):
                t.wait_ge(s_ev.h, final_ct_ev[ct])
                if ct >= 2:
                    t.wait_ge(s_ev.h, evO0 + ct - 1)
                bank = ps[2] if ct % 2 == 0 else ps[3]
                for q in range(2):
                    for j in range(4):
                        ti = 4 * q + j
                        i = t.transpose(bank[:, 512 * q + 128 * j:512 * q + 128 * (j + 1)],
                                        xr[:, ct, 128 * ti:128 * (ti + 1)], eye_sb[:])
                s_mm.inc(i)

        @block.vector
        def _(v):
            for ct in range(NG):
                v.wait_ge(s_mm.h, mmO0 + ct + 1)
                if ct >= 2:
                    v.wait_ge(s_out.h, 16 * (ct - 1))
                bank = ps[2] if ct % 2 == 0 else ps[3]
                i = v.tensor_copy(ob_v[:, ct % 2, :, :],
                                  bank[:].rearrange("p (ti j) -> p ti j", ti=8))
                s_ev.inc(i)

        @block.sync
        def _(sync):
            for ct in range(NG):
                sync.wait_ge(s_ev.h, evO0 + ct + 1)
                i = sync.dma_start(
                    out=out_d.rearrange("(ti p) c -> p ti c", p=128)[:, :, 128 * ct:128 * (ct + 1)],
                    in_=ob_v[:, ct % 2, :, :])
                s_out.inc(i, 16)
            sync.wait_ge(s_out.h, s_out.n)

    return nc


_CACHE = {}


def kernel(**inputs):
    if "nc" not in _CACHE:
        _CACHE["nc"] = build_program()
    nc = _CACHE["nc"]

    x = np.asarray(inputs["L0_attn_x"], dtype=np.float32).reshape(8192, 2048)
    eye = np.eye(128, dtype=np.float32)
    bce = np.zeros((NE, NE * 128), np.float32)
    for e in range(NE):
        bce[e, 128 * e:128 * (e + 1)] = 1.0
    in_maps = []
    for c in range(NCORES):
        fl = np.zeros((128, 2), np.float32)
        fl[:, 0] = 1.0 if c == 0 else 0.0
        fl[:, 1] = 0.0 if c == 0 else 1.0
        oh = np.zeros((128, AG, NCORES), np.float32)
        oh[:, :, c] = 1.0
        in_maps.append({
            "x": np.ascontiguousarray(x[T * c:T * (c + 1)]),
            "w_router": np.asarray(inputs["L0_ffn_w_router"], np.float32),
            "w1": np.asarray(inputs["L0_ffn_w1"], np.float32),
            "w3": np.asarray(inputs["L0_ffn_w3"], np.float32),
            "w2": np.asarray(inputs["L0_ffn_w2"], np.float32),
            "wi": np.asarray(inputs["L0_attn_input_gate_weight"], np.float32),
            "bi": np.asarray(inputs["L0_attn_input_gate_bias"], np.float32),
            "wr": np.asarray(inputs["L0_attn_recurrent_gate_weight"], np.float32),
            "br": np.asarray(inputs["L0_attn_recurrent_gate_bias"], np.float32),
            "rp": np.asarray(inputs["L0_attn_recurrent_param"], np.float32).reshape(NH, 128),
            "eye": eye,
            "flag0": fl,
            "onehot": oh,
            "bce": bce,
        })
    res = run_bass_kernel_spmd(nc, in_maps, core_ids=list(range(NCORES)))
    out = np.concatenate([res.results[c]["out"] for c in range(NCORES)], axis=0)
    return out.reshape(1, 8192, 2048).astype(np.float32)


# version 14
# speedup vs baseline: 1.0383x; 1.0383x over previous
import contextlib

import numpy as np
import ml_dtypes

import concourse.bass as bass
import concourse.bass_utils as _bu
import concourse.mybir as mybir
from concourse.bass_utils import run_bass_kernel_spmd

if not getattr(_bu, "_ldwopt_patched", False):
    _orig_run_command = _bu.run_command

    def _run_command_ldwopt(argv, **kw):
        argv = ["--enable-ldw-opt=true" if a == "--enable-ldw-opt=false" else a
                for a in argv]
        return _orig_run_command(argv, **kw)

    _bu.run_command = _run_command_ldwopt
    _bu._ldwopt_patched = True
    try:
        from concourse.compiler_utils import get_compiler_flags, set_compiler_flags
        set_compiler_flags([f.replace("--enable-ldw-opt=false",
                                      "--enable-ldw-opt=true")
                            for f in get_compiler_flags()])
    except Exception:
        pass

F32 = mybir.dt.float32
BF16 = mybir.dt.bfloat16
AF = mybir.ActivationFunctionType
ALU = mybir.AluOpType
AXX = mybir.AxisListType.X

NCORES = 8
T = 1024
D = 2048
NG = 16
NE = 8
NFT = 8
NH = 16
EG = 2
NGRP = NE // EG
AG = 2
NAG = NG // AG
EPS = 1e-5


def build_program(upto="all"):
    import os as _os
    STRIP_H = _os.environ.get("STRIP_H", "0") == "1"
    STRIP_SILU = _os.environ.get("STRIP_SILU", "0") == "1"
    STRIP_DMA = _os.environ.get("STRIP_DMA", "0") == "1"
    nc = bass.Bass("TRN2", target_bir_lowering=False)

    x_in = nc.dram_tensor("x", [T, D], F32, kind="ExternalInput").ap()
    wro_d = nc.dram_tensor("w_router", [D, NE], F32, kind="ExternalInput").ap()
    w1_d = nc.dram_tensor("w1", [NE, D, NFT * 128], F32, kind="ExternalInput").ap()
    w3_d = nc.dram_tensor("w3", [NE, D, NFT * 128], F32, kind="ExternalInput").ap()
    w2_d = nc.dram_tensor("w2", [NE, NFT * 128, D], F32, kind="ExternalInput").ap()
    wi_d = nc.dram_tensor("wi", [NH, 128, 128], F32, kind="ExternalInput").ap()
    bi_d = nc.dram_tensor("bi", [NH, 128], F32, kind="ExternalInput").ap()
    wr_d = nc.dram_tensor("wr", [NH, 128, 128], F32, kind="ExternalInput").ap()
    br_d = nc.dram_tensor("br", [NH, 128], F32, kind="ExternalInput").ap()
    rp_d = nc.dram_tensor("rp", [NH, 128], F32, kind="ExternalInput").ap()
    eye_d = nc.dram_tensor("eye", [128, 128], F32, kind="ExternalInput").ap()
    flag_d = nc.dram_tensor("flag0", [128, 2], F32, kind="ExternalInput").ap()
    oneh_d = nc.dram_tensor("onehot", [128, AG, NCORES], F32, kind="ExternalInput").ap()
    bce_d = nc.dram_tensor("bce", [NE, NE * 128], F32, kind="ExternalInput").ap()
    out_d = nc.dram_tensor("out", [T, D], F32, kind="ExternalOutput").ap()

    ccin = [nc.dram_tensor(f"ccin{g}", [128, 2 * AG], F32, kind="Internal").ap()
            for g in range(NAG)]
    ccout = [nc.dram_tensor(f"ccout{g}", [NCORES * 128, 2 * AG], F32, kind="Internal").ap()
             for g in range(NAG)]

    st = contextlib.ExitStack()
    with st:
        def sb(name, shape, dt=F32):
            return st.enter_context(nc.sbuf_tensor(name, list(shape), dt))

        xr = sb("xr", (128, NG, T))
        zb = sb("zb", (128, NG, T), BF16)
        AR = sb("AR", (128, 24576))
        sg = sb("sg", (128, T))
        eye_sb = sb("eye_sb", (128, 128))
        bce_sb = sb("bce_sb", (NE, NE * 128), BF16)
        wro_sb = sb("wro_sb", (128, NG, NE), BF16)
        ones_bf = sb("ones_bf", (128, 1), BF16)
        ones_row = sb("ones_row", (1, 128))
        bias_i = sb("bias_i", (128, NH))
        bias_r = sb("bias_r", (128, NH))
        msp = sb("msp", (128, NH))
        tmp16 = sb("tmp16", (128, NH))
        brow = sb("brow", (128, NH))
        flag = sb("flag", (128, 2))
        oneh = sb("oneh", (128, AG, NCORES))
        lt_sb = sb("lt_sb", (128, NE, NE))
        wts_sb = sb("wts_sb", (128, NE, NE))
        m_sc = sb("m_sc", (128, NE))
        m2_sc = sb("m2_sc", (128, NE))
        t8a = sb("t8a", (128, NE, NE))
        cin_sb = sb("cin_sb", (128, 2 * AG))
        agb = [sb(f"agb{b}", (128, NCORES, 2 * AG)) for b in range(2)]
        cpre = sb("cpre", (128, AG, 9))
        csel = sb("csel", (128, AG, NCORES))
        myc = sb("myc", (128, AG, 1))
        ceps = sb("ceps", (128, 1))
        sgp = sb("sgp", (128, T))

        def arf(lo, hi):
            return AR[:, lo:hi]

        def arb(lo, hi):
            return AR[:, lo:hi].bitcast(BF16)

        wbc_v = arb(0, 4096).rearrange("p (e t) -> p e t", e=NE)
        stg_v = arf(0, 4096).rearrange("p (b c) -> p b c", b=2)
        h_v = arb(4096, 12288).rearrange("p (e f t) -> p e f t", e=EG, f=NFT)
        w13_v = arb(12288, 20480).rearrange(
            "p (b w gc j) -> p b w gc j", b=4, w=2, gc=NG)
        w2c_v = arb(20480, 24576).rearrange(
            "p (b e gf c) -> p b e gf c", b=4, e=EG, gf=NFT)
        R0 = AR[0:1, 4096:5120]
        R1 = AR[0:1, 5120:6144]
        R2 = AR[0:1, 6144:7168]
        xb_v = arb(8192, 9216).rearrange("p (b t) -> p b t", b=2)
        sq_v = arb(9216, 10240).rearrange("p (b t) -> p b t", b=2)
        wT_v = AR[0:NE, 4096:5120]
        wT2_v = AR[0:NE, 5120:5632].bitcast(BF16)
        z32_v = arf(0, 2048).rearrange("p (i t) -> p i t", i=AG)
        gx_v = arf(2048, 4096).rearrange("p (i t) -> p i t", i=AG)
        sa_v = arf(4096, 6144).rearrange("p (i t) -> p i t", i=AG)
        a_v = arf(6144, 8192).rearrange("p (i t) -> p i t", i=AG)
        ml_v = arf(8192, 10240).rearrange("p (i t) -> p i t", i=AG)
        y_v = [arf(10240, 12288).rearrange("p (i t) -> p i t", i=AG),
               arf(12288, 14336).rearrange("p (i t) -> p i t", i=AG)]
        ac_v = [arf(14336, 16384).rearrange("p (i t) -> p i t", i=AG),
                arf(16384, 18432).rearrange("p (i t) -> p i t", i=AG)]
        ab_v = arf(18432, 19456)
        bb_v = arf(19456, 20480)
        wi_v = arb(20480, 21504).rearrange("p (h j) -> p h j", h=NH)
        wr_v = arb(21504, 22528).rearrange("p (h j) -> p h j", h=NH)
        zeros_t = arf(22528, 23552)
        ob_v = zb[:, 0:4, :].bitcast(F32).rearrange(
            "p a t -> p (a t)").rearrange("p (b ti j) -> p b ti j", b=2, ti=NCORES)
        scr_v = zb[:, 15, :].bitcast(F32)

        ps = [st.enter_context(nc.psum_tensor(f"ps{i}", [128, T], F32)) for i in range(4)]

        class Ctr:
            def __init__(self, name):
                self.h = st.enter_context(nc.semaphore(name))
                self.n = 0

            def inc(self, inst, k=1):
                inst.then_inc(self.h, k)
                self.n += k
                return inst

        s_in = Ctr("s_in")
        s_w = Ctr("s_w")
        s_mm = Ctr("s_mm")
        s_sc = Ctr("s_sc")
        s_ev = Ctr("s_ev")
        s_gp = Ctr("s_gp")
        s_cc = Ctr("s_cc")
        s_out = Ctr("s_out")
        s_in2 = Ctr("s_in2")

        block = st.enter_context(nc.Block())

        final_ct_ev = [None] * NG
        MM_GATES_LAST = [0]

        @block.sync
        def _(sync):
            s_in.inc(sync.dma_start(out=eye_sb[:], in_=eye_d[:]), 16)
            s_in.inc(sync.dma_start(out=flag[:], in_=flag_d[:]), 16)
            s_in.inc(sync.dma_start(out=oneh[:], in_=oneh_d[:]), 16)
            s_in.inc(sync.dma_start(out=scr_v[0:16, 0:128], in_=bi_d[:]), 16)
            s_in.inc(sync.dma_start(out=scr_v[0:16, 128:256], in_=br_d[:]), 16)
            s_in.inc(sync.dma_start(out=scr_v[0:16, 256:384], in_=rp_d[:]), 16)
        SIN_CONST = s_in.n

        @block.gpsimd
        def _(g):
            s_w.inc(g.dma_start(out=bce_sb[:], in_=bce_d[:]), 16)
            s_w.inc(g.dma_start(out=wro_sb[:],
                                in_=wro_d.rearrange("(gc p) e -> p gc e", p=128)), 16)
        SW_CONST = s_w.n

        @block.sync
        def _(sync):
            for ti in range(0, 8, 2):
                if ti >= 2:
                    sync.wait_ge(s_ev.h, 4 * (ti - 1))
                s_in.inc(sync.dma_start(out=stg_v[:, 0, :],
                                        in_=x_in[128 * ti:128 * (ti + 1), :]), 16)

        @block.scalar
        def _(sc):
            for ti in range(1, 8, 2):
                if ti >= 2:
                    sc.wait_ge(s_ev.h, 4 * (ti - 1))
                s_in2.inc(sc.dma_start(out=stg_v[:, 1, :],
                                       in_=x_in[128 * ti:128 * (ti + 1), :]), 16)

        @block.tensor
        def _(t):
            t.wait_ge(s_in.h, SIN_CONST)
            t.transpose(ps[0][:, 0:16], scr_v[0:16, 0:128], eye_sb[0:16, 0:16])
            t.transpose(ps[0][:, 16:32], scr_v[0:16, 128:256], eye_sb[0:16, 0:16])
            i = t.transpose(ps[0][:, 32:48], scr_v[0:16, 256:384], eye_sb[0:16, 0:16])
            s_mm.inc(i)
            for ti in range(8):
                if ti % 2 == 0:
                    t.wait_ge(s_in.h, SIN_CONST + 16 * (ti // 2 + 1))
                else:
                    t.wait_ge(s_in2.h, 16 * (ti // 2 + 1))
                for q in range(4):
                    Q = 4 * ti + q
                    if Q >= 4:
                        t.wait_ge(s_ev.h, Q - 3)
                    bank = ps[1 + Q % 2]
                    off = 512 * ((Q % 4) // 2)
                    for j in range(4):
                        gi = 4 * q + j
                        i = t.transpose(bank[:, off + 128 * j:off + 128 * (j + 1)],
                                        stg_v[:, ti % 2, 128 * gi:128 * (gi + 1)],
                                        eye_sb[:])
                    s_mm.inc(i)
        MM_IN = s_mm.n
        assert MM_IN == 33

        @block.vector
        def _(v):
            v.wait_ge(s_mm.h, 1)
            v.tensor_copy(bias_i[:], ps[0][:, 0:16])
            v.tensor_copy(bias_r[:], ps[0][:, 16:32])
            v.tensor_copy(brow[:], ps[0][:, 32:48])
            v.drain()
            v.tensor_single_scalar(tmp16[:], brow[:], 0.0, ALU.is_ge)
            v.drain()
            v.tensor_scalar(tmp16[:], tmp16[:], 2.0, -1.0, ALU.mult, ALU.add)
            v.drain()
            v.tensor_mul(tmp16[:], brow[:], tmp16[:])
            v.drain()
            v.memset(cpre[:], 0.0)
            v.memset(ones_bf[:], 1.0)
            v.memset(ones_row[:], 1.0)
            v.memset(ceps[:], EPS)
            for Q in range(32):
                ti, q = Q // 4, Q % 4
                v.wait_ge(s_mm.h, 1 + Q + 1)
                bank = ps[1 + Q % 2]
                off = 512 * ((Q % 4) // 2)
                i = v.tensor_copy(
                    xr[:, 4 * q:4 * q + 4, 128 * ti:128 * (ti + 1)],
                    bank[:, off:off + 512].rearrange("p (j c) -> p j c", j=4))
                s_ev.inc(i)
        EV_IN = s_ev.n
        assert EV_IN == 32

        @block.scalar
        def _(s):
            s.wait_ge(s_ev.h, EV_IN)
            s.activation(msp[:], tmp16[:], AF.Exp, scale=-1.0)
            s.activation(tmp16[:], msp[:], AF.Ln, bias=1.0)
            i = s.drain()
            s_sc.inc(i)
        SC_MSP = s_sc.n

        @block.vector
        def _(v):
            v.wait_ge(s_sc.h, SC_MSP)
            v.tensor_relu(msp[:], brow[:])
            v.drain()
            v.tensor_add(msp[:], msp[:], tmp16[:])
            v.drain()
            v.tensor_scalar_mul(msp[:], msp[:], -8.0)
            i = v.drain()
            s_ev.inc(i)

        def emit_ln(gate_ev_scalar):
            mm0, sc0, ev0 = s_mm.n, s_sc.n, s_ev.n

            @block.vector
            def _(v):
                for g in range(NG):
                    if g >= 2:
                        v.wait_ge(s_mm.h, mm0 + g - 1)
                    i = v.tensor_copy(xb_v[:, g % 2, :], xr[:, g, :])
                    s_ev.inc(i)

            @block.scalar
            def _(s):
                s.wait_ge(s_ev.h, gate_ev_scalar)
                for g in range(NG):
                    if g >= 2:
                        s.wait_ge(s_mm.h, mm0 + g - 1)
                    i = s.activation(sq_v[:, g % 2, :], xr[:, g, :], AF.Square)
                    s_sc.inc(i)

            @block.tensor
            def _(t):
                for g in range(NG):
                    t.wait_ge(s_ev.h, ev0 + g + 1)
                    t.wait_ge(s_sc.h, sc0 + g + 1)
                    for half in range(2):
                        sl = slice(512 * half, 512 * half + 512)
                        t.matmul(ps[2][0:1, sl], ones_bf[:], xb_v[:, g % 2, sl],
                                 start=(g == 0), stop=(g == NG - 1),
                                 skip_group_check=True)
                        i = t.matmul(ps[3][0:1, sl], ones_bf[:], sq_v[:, g % 2, sl],
                                     start=(g == 0), stop=(g == NG - 1),
                                     skip_group_check=True)
                    s_mm.inc(i)
            mm_stats = s_mm.n

            @block.vector
            def _(v):
                v.wait_ge(s_mm.h, mm_stats)
                v.tensor_scalar_mul(R0, ps[2][0:1, :], 1.0 / D)
                v.tensor_scalar_mul(R1, ps[3][0:1, :], 1.0 / D)
                v.drain()
                v.tensor_mul(R2, R0, R0)
                v.drain()
                v.tensor_sub(R1, R1, R2)
                i = v.drain()
                s_ev.inc(i)
            ev_var = s_ev.n

            @block.scalar
            def _(s):
                s.wait_ge(s_ev.h, ev_var)
                s.activation(R2, R1, AF.Ln, bias=ceps[0:1, 0:1])
                s.activation(R1, R2, AF.Exp, scale=-0.5)
                i = s.drain()
                s_sc.inc(i)
            sc_alpha = s_sc.n

            @block.vector
            def _(v):
                v.wait_ge(s_sc.h, sc_alpha)
                v.scalar_tensor_tensor(R2, R0, -1.0, R1, ALU.mult, ALU.mult)
                i = v.drain()
                s_ev.inc(i)
            ev_beta = s_ev.n

            @block.tensor
            def _(t):
                t.wait_ge(s_ev.h, ev_beta)
                for half in range(2):
                    sl = slice(512 * half, 512 * half + 512)
                    t.matmul(ps[0][:, sl], ones_row[:], R1[0:1, 512 * half:512 * half + 512],
                             start=True, stop=True)
                    i = t.matmul(ps[1][:, sl], ones_row[:], R2[0:1, 512 * half:512 * half + 512],
                                 start=True, stop=True)
                s_mm.inc(i)
            mm_bc = s_mm.n

            @block.vector
            def _(v):
                v.wait_ge(s_mm.h, mm_bc)
                v.tensor_copy(ab_v, ps[0][:])
                i = v.tensor_copy(bb_v, ps[1][:])
                s_ev.inc(i)
            return s_ev.n

        def emit_ln_apply_all(ev_ab):
            ev0 = s_ev.n

            @block.vector
            def _(v):
                v.wait_ge(s_ev.h, ev_ab)
                for g in range(NG):
                    v.tensor_mul(sg[:], xr[:, g, :], ab_v)
                    i = v.tensor_add(zb[:, g, :], sg[:], bb_v)
                    s_ev.inc(i)
            return ev0

        def emit_router(ev_z0):
            @block.tensor
            def _(t):
                t.wait_ge(s_w.h, SW_CONST)
                for g in range(NG):
                    t.wait_ge(s_ev.h, ev_z0 + g + 1)
                    for half in range(2):
                        sl = slice(512 * half, 512 * half + 512)
                        i = t.matmul(ps[2][0:8, sl], wro_sb[:, g, :], zb[:, g, sl],
                                     start=(g == 0), stop=(g == NG - 1),
                                     skip_group_check=True)
                    s_mm.inc(i)
            mm_lg = s_mm.n

            @block.vector
            def _(v):
                v.wait_ge(s_mm.h, mm_lg)
                i = v.tensor_copy(wT_v, ps[2][0:8, :])
                s_ev.inc(i)
            ev_lg = s_ev.n

            @block.tensor
            def _(t):
                t.wait_ge(s_ev.h, ev_lg)
                for j in range(8):
                    i = t.transpose(ps[3][:, 8 * j:8 * (j + 1)],
                                    wT_v[0:8, 128 * j:128 * (j + 1)], eye_sb[0:8, 0:8])
                s_mm.inc(i)
            mm_tp = s_mm.n

            @block.vector
            def _(v):
                v.wait_ge(s_mm.h, mm_tp)
                v.tensor_copy(lt_sb[:].rearrange("p a b -> p (a b)"), ps[3][:, 0:64])
                v.drain()
                for j in range(8):
                    v.reduce_max(m_sc[:, j:j + 1], lt_sb[:, j, :], axis=AXX, negate=True)
                v.drain()
                v.tensor_scalar_mul(m2_sc[:], m_sc[:], -1.0)
                i = v.drain()
                s_ev.inc(i)
            ev_m1 = s_ev.n

            @block.scalar
            def _(s):
                s.wait_ge(s_ev.h, ev_m1)
                for j in range(8):
                    s.activation(wts_sb[:, j, :], lt_sb[:, j, :], AF.Exp,
                                 bias=m_sc[:, j:j + 1])
                i = s.drain()
                s_sc.inc(i)
            sc_exp = s_sc.n

            @block.vector
            def _(v):
                v.wait_ge(s_sc.h, sc_exp)
                for j in range(8):
                    v.tensor_single_scalar(t8a[:, j, :], lt_sb[:, j, :],
                                           m2_sc[:, j:j + 1], ALU.is_ge)
                v.drain()
                for j in range(8):
                    v.scalar_tensor_tensor(t8a[:, j, :], t8a[:, j, :], -1e30,
                                           lt_sb[:, j, :], ALU.mult, ALU.add)
                v.drain()
                for j in range(8):
                    v.reduce_max(m2_sc[:, j:j + 1], t8a[:, j, :], axis=AXX)
                v.drain()
                for j in range(8):
                    v.tensor_single_scalar(t8a[:, j, :], lt_sb[:, j, :],
                                           m2_sc[:, j:j + 1], ALU.is_ge)
                v.drain()
                for j in range(8):
                    v.tensor_mul(wts_sb[:, j, :], wts_sb[:, j, :], t8a[:, j, :])
                v.drain()
                for j in range(8):
                    v.reduce_sum(m2_sc[:, j:j + 1], wts_sb[:, j, :], axis=AXX)
                v.drain()
                v.reciprocal(m2_sc[:], m2_sc[:])
                v.drain()
                v.tensor_scalar_mul(m2_sc[:], m2_sc[:], 0.5)
                v.drain()
                for j in range(8):
                    v.tensor_single_scalar(wts_sb[:, j, :], wts_sb[:, j, :],
                                           m2_sc[:, j:j + 1], ALU.mult)
                i = v.drain()
                s_ev.inc(i)
            ev_w = s_ev.n

            @block.tensor
            def _(t):
                t.wait_ge(s_ev.h, ev_w)
                for j in range(8):
                    i = t.transpose(ps[3][0:8, 128 * j:128 * (j + 1)],
                                    wts_sb[:, j, :], eye_sb[:])
                s_mm.inc(i)
            mm_tpb = s_mm.n

            @block.vector
            def _(v):
                v.wait_ge(s_mm.h, mm_tpb)
                i = v.tensor_copy(wT2_v, ps[3][0:8, :])
                s_ev.inc(i)
            ev_wT2 = s_ev.n

            @block.tensor
            def _(t):
                t.wait_ge(s_ev.h, ev_wT2)
                for e in range(NE):
                    if e >= 2:
                        t.wait_ge(s_ev.h, ev_wT2 + e - 1)
                    p = ps[e % 2]
                    for half in range(2):
                        sl = slice(512 * half, 512 * half + 512)
                        i = t.matmul(p[:, sl],
                                     bce_sb[0:8, 128 * e:128 * (e + 1)],
                                     wT2_v[0:8, sl], start=True, stop=True,
                                     skip_group_check=True)
                    s_mm.inc(i)

            @block.vector
            def _(v):
                for e in range(NE):
                    v.wait_ge(s_mm.h, mm_tpb + e + 1)
                    i = v.tensor_copy(wbc_v[:, e, :], ps[e % 2][:])
                    s_ev.inc(i)
            return s_ev.n

        def emit_ffn(call):
            for grp in range(NGRP):
                e0 = EG * grp
                mmG0, scG0, evG0, swG0 = s_mm.n, s_sc.n, s_ev.n, s_w.n

                @block.gpsimd
                def _(g):
                    for c in range(16):
                        ei, fc = c // 8, c % 8
                        if c < 4:
                            if grp == 0:
                                g.wait_ge(s_ev.h, evG0)
                            else:
                                g.wait_ge(s_mm.h, mmG0 - 16)
                        else:
                            g.wait_ge(s_mm.h, mmG0 + c - 3)
                        ngc = 1 if STRIP_DMA else 8
                        for wsel, wd in ((0, w1_d), (1, w3_d)):
                            src = wd[e0 + ei].rearrange("(gc p) f -> p gc f", p=128)
                            for gh in range(2):
                                i = g.dma_start(
                                    out=w13_v[:, c % 4, wsel, 8 * gh:8 * gh + ngc, :],
                                    in_=src[:, 8 * gh:8 * gh + ngc, 128 * fc:128 * fc + 128])
                                s_w.inc(i, 16)

                @block.tensor
                def _(t):
                    for f in range(16):
                        ei, ft = f // 8, f % 8
                        c = f
                        t.wait_ge(s_w.h, swG0 + 64 * (c + 1))
                        if f == 0:
                            t.wait_ge(s_ev.h, evG0 if grp == 0 else evG0 - 17)
                        elif f == 1:
                            t.wait_ge(s_ev.h, evG0)
                        else:
                            t.wait_ge(s_ev.h, evG0 + f - 1)
                        pg = ps[2] if f % 2 == 0 else ps[0]
                        pu = ps[3] if f % 2 == 0 else ps[1]
                        for wsel, ptgt in ((0, pg), (1, pu)):
                            for half in range(2):
                                sl = slice(512 * half, 512 * half + 512)
                                for gc in range(NG):
                                    i = t.matmul(ptgt[:, sl],
                                                 w13_v[:, c % 4, wsel, gc, :],
                                                 zb[:, gc, sl],
                                                 start=(gc == 0), stop=(gc == NG - 1))
                        s_mm.inc(i)

                @block.scalar
                def _(s):
                    for f in range(16):
                        s.wait_ge(s_mm.h, mmG0 + f + 1)
                        pg = ps[2] if f % 2 == 0 else ps[0]
                        if STRIP_SILU:
                            i = s.copy(m_sc[:, 0:1], m_sc[:, 0:1])
                        else:
                            i = s.activation(pg[:], pg[:], AF.Silu)
                        s_sc.inc(i)

                @block.vector
                def _(v):
                    for f in range(16):
                        ei, ft = f // 8, f % 8
                        v.wait_ge(s_sc.h, scG0 + f + 1)
                        if f >= 1:
                            v.wait_ge(s_ev.h, evG0 + f)
                        pg = ps[2] if f % 2 == 0 else ps[0]
                        pu = ps[3] if f % 2 == 0 else ps[1]
                        if STRIP_H:
                            i = v.tensor_copy(m2_sc[:, 0:1], m2_sc[:, 0:1])
                        else:
                            v.tensor_mul(sg[:], pg[:], wbc_v[:, e0 + ei, :])
                            i = v.tensor_mul(h_v[:, ei, ft, :], sg[:], pu[:])
                        s_ev.inc(i)

                mmY0, evY0, swY0 = s_mm.n, s_ev.n, s_w.n

                @block.gpsimd
                def _(g):
                    for tch in range(16):
                        if tch < 4:
                            if grp == 0 and call == 1:
                                g.wait_ge(s_ev.h, evY0 - 16)
                            elif grp > 0:
                                g.wait_ge(s_mm.h, mmY0 - 16)
                        else:
                            g.wait_ge(s_mm.h, mmY0 + tch - 3)
                        ngf = 1 if STRIP_DMA else 8
                        for ei in range(EG):
                            src = w2_d[e0 + ei].rearrange("(gf p) c -> p gf c", p=128)
                            i = g.dma_start(
                                out=w2c_v[:, tch % 4, ei, 0:ngf, :],
                                in_=src[:, 0:ngf, 128 * tch:128 * tch + 128])
                            s_w.inc(i, 16)

                @block.tensor
                def _(t):
                    for ct in range(NG):
                        tch = ct
                        t.wait_ge(s_w.h, swY0 + 32 * (tch + 1))
                        if ct < 2:
                            t.wait_ge(s_ev.h, evY0)
                        else:
                            t.wait_ge(s_ev.h, evY0 + ct - 1)
                        py = ps[0] if ct % 2 == 0 else ps[1]
                        for half in range(2):
                            sl = slice(512 * half, 512 * half + 512)
                            n = 0
                            for ei in range(EG):
                                for gf in range(NFT):
                                    i = t.matmul(py[:, sl],
                                                 w2c_v[:, tch % 4, ei, gf, :],
                                                 h_v[:, ei, gf, sl],
                                                 start=(n == 0), stop=(n == EG * NFT - 1))
                                    n += 1
                        s_mm.inc(i)

                @block.vector
                def _(v):
                    for ct in range(NG):
                        v.wait_ge(s_mm.h, mmY0 + ct + 1)
                        py = ps[0] if ct % 2 == 0 else ps[1]
                        i = v.tensor_add(xr[:, ct, :], xr[:, ct, :], py[:])
                        s_ev.inc(i)
                        if call == 1 and grp == NGRP - 1:
                            final_ct_ev[ct] = s_ev.n

        def emit_attn(ev_ab):
            MM_FFN1 = s_mm.n

            @block.gpsimd
            def _(g):
                g.wait_ge(s_mm.h, MM_FFN1)
                s_w.inc(g.dma_start(out=wi_v, in_=wi_d.rearrange("h d c -> d h c")), 16)
                s_w.inc(g.dma_start(out=wr_v, in_=wr_d.rearrange("h d c -> d h c")), 16)
            SW_WIWR = s_w.n

            @block.vector
            def _(v):
                v.wait_ge(s_ev.h, ev_ab)
                v.wait_ge(s_mm.h, MM_FFN1)
                v.memset(zeros_t, 0.0)
                for g in range(4):
                    v.tensor_mul(sg[:], xr[:, g, :], ab_v)
                    i = v.tensor_add(zb[:, g, :], sg[:], bb_v)
                    s_ev.inc(i)
            ev_zb_first = s_ev.n
            gp_zb = [0] * NAG

            fix = [None] * NAG
            sc_sig_prev = [0, 0]
            ev_scan_prev = [0]
            ev_nx2_prev = [0]

            def emit_fixup(k):
                b, gA, gp_after = fix[k]

                @block.vector
                def _(v):
                    v.wait_ge(s_gp.h, gp_after)
                    for r in range(NCORES):
                        for i2 in range(AG):
                            v.scalar_tensor_tensor(
                                cpre[:, i2, r + 1:r + 2],
                                agb[b][:, r, 2 * i2:2 * i2 + 1],
                                cpre[:, i2, r:r + 1],
                                agb[b][:, r, 2 * i2 + 1:2 * i2 + 2],
                                ALU.mult, ALU.add)
                        v.drain()
                    v.tensor_mul(csel[:], cpre[:, :, 0:8], oneh[:])
                    v.drain()
                    v.reduce_sum(myc[:], csel[:], axis=AXX)
                    v.drain()
                    for i2 in range(AG):
                        g2 = gA + i2
                        v.scalar_tensor_tensor(y_v[b][:, i2, :], ac_v[b][:, i2, :],
                                               myc[:, i2, 0:1], y_v[b][:, i2, :],
                                               ALU.mult, ALU.add)
                        inst = v.tensor_add(xr[:, g2, :], xr[:, g2, :], y_v[b][:, i2, :])
                    s_ev.inc(inst)

            for k in range(NAG):
                b = k % 2
                gA = AG * k
                ev_apply0 = s_ev.n

                @block.vector
                def _(v):
                    for i2 in range(AG):
                        g2 = gA + i2
                        v.tensor_mul(sg[:], xr[:, g2, :], ab_v)
                        inst = v.tensor_add(z32_v[:, i2, :], sg[:], bb_v)
                        s_ev.inc(inst)

                mmK0 = s_mm.n

                @block.tensor
                def _(t):
                    if k == 0:
                        t.wait_ge(s_w.h, SW_WIWR)
                        t.wait_ge(s_ev.h, ev_zb_first)
                    if k >= 2:
                        t.wait_ge(s_gp.h, gp_zb[k])
                    for i2 in range(AG):
                        g2 = gA + i2
                        if k > 0:
                            t.wait_ge(s_sc.h, sc_sig_prev[i2])
                        pgx, pga = ps[2 * i2], ps[2 * i2 + 1]
                        for half in range(2):
                            sl = slice(512 * half, 512 * half + 512)
                            t.matmul(pgx[:, sl], wi_v[:, g2, :], zb[:, g2, sl],
                                     start=True, stop=True, skip_group_check=True)
                            inst = t.matmul(pga[:, sl], wr_v[:, g2, :], zb[:, g2, sl],
                                            start=True, stop=True, skip_group_check=True)
                        s_mm.inc(inst)
                MM_GATES_LAST[0] = s_mm.n

                sc_sig = [0, 0]

                @block.scalar
                def _(s):
                    if k > 0:
                        s.wait_ge(s_ev.h, ev_nx2_prev[0])
                    for i2 in range(AG):
                        g2 = gA + i2
                        s.wait_ge(s_mm.h, mmK0 + i2 + 1)
                        s.activation(gx_v[:, i2, :], ps[2 * i2][:], AF.Sigmoid,
                                     bias=bias_i[:, g2:g2 + 1])
                        inst = s.activation(sa_v[:, i2, :], ps[2 * i2 + 1][:], AF.Sigmoid,
                                            bias=bias_r[:, g2:g2 + 1])
                        s_sc.inc(inst)
                        sc_sig[i2] = s_sc.n
                sc_sig_prev = sc_sig

                @block.scalar
                def _(s):
                    if k > 0:
                        s.wait_ge(s_ev.h, ev_scan_prev[0])
                    for i2 in range(AG):
                        g2 = gA + i2
                        s.activation(a_v[:, i2, :], sa_v[:, i2, :], AF.Exp,
                                     scale=msp[:, g2:g2 + 1])
                    i = s.drain()
                    s_sc.inc(i)
                sc_a = s_sc.n

                @block.scalar
                def _(s):
                    fa = a_v.rearrange("p i t -> p (i t)")
                    fs = sa_v.rearrange("p i t -> p (i t)")
                    fm = ml_v.rearrange("p i t -> p (i t)")
                    s.activation(fm, fa, AF.Square)
                    inst = s.activation(fs, fm, AF.Relu, scale=-1.0, bias=1.0)
                    s_sc.inc(inst)
                sc_u = s_sc.n

                @block.vector
                def _(v):
                    v.wait_ge(s_sc.h, sc_sig[AG - 1])
                    fz = z32_v.rearrange("p i t -> p (i t)")
                    fg = gx_v.rearrange("p i t -> p (i t)")
                    inst = v.tensor_mul(fg, fz, fg)
                    s_ev.inc(inst)

                @block.scalar
                def _(s):
                    fs = sa_v.rearrange("p i t -> p (i t)")
                    fm = ml_v.rearrange("p i t -> p (i t)")
                    s.activation(fm, fs, AF.Ln)
                    inst = s.activation(fs, fm, AF.Exp, scale=0.5)
                    s_sc.inc(inst)
                sc_mult = s_sc.n

                @block.vector
                def _(v):
                    v.wait_ge(s_sc.h, sc_mult)
                    for i2 in range(AG):
                        v.scalar_tensor_tensor(sa_v[:, i2, 0:1], sa_v[:, i2, 0:1],
                                               flag[:, 1:2], flag[:, 0:1],
                                               ALU.mult, ALU.add)
                    v.drain()
                    fg = gx_v.rearrange("p i t -> p (i t)")
                    fs = sa_v.rearrange("p i t -> p (i t)")
                    fm = ml_v.rearrange("p i t -> p (i t)")
                    i = v.tensor_mul(fm, fg, fs)
                    s_ev.inc(i)
                    for i2 in range(AG):
                        v.tensor_tensor_scan(y_v[b][:, i2, :], a_v[:, i2, :],
                                             ml_v[:, i2, :], 0.0, ALU.mult, ALU.add)
                        v.tensor_tensor_scan(ac_v[b][:, i2, :], a_v[:, i2, :],
                                             zeros_t[:], 1.0, ALU.mult, ALU.add)
                    v.drain()
                    for i2 in range(AG):
                        v.tensor_copy(cin_sb[:, 2 * i2:2 * i2 + 1],
                                      ac_v[b][:, i2, T - 1:T])
                        v.tensor_copy(cin_sb[:, 2 * i2 + 1:2 * i2 + 2],
                                      y_v[b][:, i2, T - 1:T])
                    inst = v.drain()
                    s_ev.inc(inst)
                ev_scan = s_ev.n
                ev_nx2_prev[0] = ev_scan - 1
                ev_scan_prev[0] = ev_scan

                gp0 = s_gp.n

                @block.gpsimd
                def _(g):
                    if k == 0:
                        g.wait_ge(s_ev.h, ev_ab)
                        for g2 in range(4, 8):
                            g.tensor_mul(sgp[:], xr[:, g2, :], ab_v)
                            i = g.tensor_add(zb[:, g2, :], sgp[:], bb_v)
                            s_gp.inc(i, 1)
                        gp_zb[2] = gp_zb[3] = s_gp.n
                    elif 1 <= k <= 4:
                        for g2 in range(2 * k + 6, 2 * k + 8):
                            g.tensor_mul(sgp[:], xr[:, g2, :], ab_v)
                            i = g.tensor_add(zb[:, g2, :], sgp[:], bb_v)
                            s_gp.inc(i, 1)
                        gp_zb[k + 3] = s_gp.n
                    g.wait_ge(s_ev.h, ev_scan)
                    s_gp.inc(g.dma_start(out=ccin[k][:], in_=cin_sb[:]), 16)
                    g.wait_ge(s_gp.h, gp0 + 16)
                    s_cc.inc(g.collective_compute(
                        "AllGather", ALU.bypass,
                        replica_groups=[list(range(NCORES))],
                        ins=[ccin[k][:].opt()], outs=[ccout[k][:].opt()]), 1)
                    g.wait_ge(s_cc.h, s_cc.n)
                    s_gp.inc(g.dma_start(
                        out=agb[b][:],
                        in_=ccout[k].rearrange("(r p) w -> p r w", p=128)), 16)
                fix[k] = (b, gA, s_gp.n)

                if k >= 1:
                    emit_fixup(k - 1)
            emit_fixup(NAG - 1)
            return s_ev.n

        dump_specs = []

        def dump(name, ap):
            dump_specs.append((name, ap))

        def finish_dumps():
            evq, mmq, scq, gpq = s_ev.n, s_mm.n, s_sc.n, s_gp.n
            outs = []
            for name, ap in dump_specs:
                shp = list(ap.shape)
                dt = ap.dtype
                d = nc.dram_tensor("dbg_" + name, shp, dt, kind="ExternalOutput").ap()
                outs.append((d, ap))

            @block.sync
            def _(sync):
                sync.wait_ge(s_ev.h, evq)
                sync.wait_ge(s_mm.h, mmq)
                sync.wait_ge(s_sc.h, scq)
                if gpq:
                    sync.wait_ge(s_gp.h, gpq)
                for d, ap in outs:
                    s_out.inc(sync.dma_start(out=d, in_=ap), 16)
                sync.wait_ge(s_out.h, s_out.n)

        if upto == "x0":
            dump("xr", xr[:])
            finish_dumps()
            return nc

        ev_ab1 = emit_ln(gate_ev_scalar=EV_IN)
        ev_z1 = emit_ln_apply_all(ev_ab1)
        if upto == "ln1":
            dump("zb", zb[:])
            dump("ab", ab_v)
            dump("bb", bb_v)
            dump("msp", msp[:])
            dump("tmp16", tmp16[:])
            dump("brow", brow[:])
            finish_dumps()
            return nc
        emit_router(ev_z1)
        if upto == "router1":
            dump("wts", wts_sb[:])
            dump("lt", lt_sb[:])
            dump("wbc", wbc_v)
            dump("wT2", wT2_v)
            finish_dumps()
            return nc
        emit_ffn(0)
        if upto == "ffn1":
            dump("xr", xr[:])
            finish_dumps()
            return nc

        ev_ab2 = emit_ln(gate_ev_scalar=s_ev.n)
        emit_attn(ev_ab2)
        if upto == "attn":
            dump("xr", xr[:])
            finish_dumps()
            return nc

        ev_ab3 = emit_ln(gate_ev_scalar=s_ev.n)
        ev_z3 = emit_ln_apply_all(ev_ab3)
        emit_router(ev_z3)
        emit_ffn(1)

        mmO0 = s_mm.n
        evO0 = s_ev.n

        @block.tensor
        def _(t):
            for ct in range(NG):
                t.wait_ge(s_ev.h, final_ct_ev[ct])
                if ct >= 2:
                    t.wait_ge(s_ev.h, evO0 + ct - 1)
                bank = ps[2] if ct % 2 == 0 else ps[3]
                for q in range(2):
                    for j in range(4):
                        ti = 4 * q + j
                        i = t.transpose(bank[:, 512 * q + 128 * j:512 * q + 128 * (j + 1)],
                                        xr[:, ct, 128 * ti:128 * (ti + 1)], eye_sb[:])
                s_mm.inc(i)

        @block.vector
        def _(v):
            for ct in range(NG):
                v.wait_ge(s_mm.h, mmO0 + ct + 1)
                if ct >= 2:
                    v.wait_ge(s_out.h, 16 * (ct - 1))
                bank = ps[2] if ct % 2 == 0 else ps[3]
                i = v.tensor_copy(ob_v[:, ct % 2, :, :],
                                  bank[:].rearrange("p (ti j) -> p ti j", ti=8))
                s_ev.inc(i)

        @block.sync
        def _(sync):
            for ct in range(NG):
                sync.wait_ge(s_ev.h, evO0 + ct + 1)
                i = sync.dma_start(
                    out=out_d.rearrange("(ti p) c -> p ti c", p=128)[:, :, 128 * ct:128 * (ct + 1)],
                    in_=ob_v[:, ct % 2, :, :])
                s_out.inc(i, 16)
            sync.wait_ge(s_out.h, s_out.n)

    return nc


_CACHE = {}


def kernel(**inputs):
    if "nc" not in _CACHE:
        _CACHE["nc"] = build_program()
    nc = _CACHE["nc"]

    x = np.asarray(inputs["L0_attn_x"], dtype=np.float32).reshape(8192, 2048)
    eye = np.eye(128, dtype=np.float32)
    bce = np.zeros((NE, NE * 128), np.float32)
    for e in range(NE):
        bce[e, 128 * e:128 * (e + 1)] = 1.0
    in_maps = []
    for c in range(NCORES):
        fl = np.zeros((128, 2), np.float32)
        fl[:, 0] = 1.0 if c == 0 else 0.0
        fl[:, 1] = 0.0 if c == 0 else 1.0
        oh = np.zeros((128, AG, NCORES), np.float32)
        oh[:, :, c] = 1.0
        in_maps.append({
            "x": np.ascontiguousarray(x[T * c:T * (c + 1)]),
            "w_router": np.asarray(inputs["L0_ffn_w_router"], np.float32),
            "w1": np.asarray(inputs["L0_ffn_w1"], np.float32),
            "w3": np.asarray(inputs["L0_ffn_w3"], np.float32),
            "w2": np.asarray(inputs["L0_ffn_w2"], np.float32),
            "wi": np.asarray(inputs["L0_attn_input_gate_weight"], np.float32),
            "bi": np.asarray(inputs["L0_attn_input_gate_bias"], np.float32),
            "wr": np.asarray(inputs["L0_attn_recurrent_gate_weight"], np.float32),
            "br": np.asarray(inputs["L0_attn_recurrent_gate_bias"], np.float32),
            "rp": np.asarray(inputs["L0_attn_recurrent_param"], np.float32).reshape(NH, 128),
            "eye": eye,
            "flag0": fl,
            "onehot": oh,
            "bce": bce,
        })
    res = run_bass_kernel_spmd(nc, in_maps, core_ids=list(range(NCORES)))
    out = np.concatenate([res.results[c]["out"] for c in range(NCORES)], axis=0)
    return out.reshape(1, 8192, 2048).astype(np.float32)
